# version 03
# baseline (speedup 1.0000x reference)
import contextlib
import numpy as np
import ml_dtypes
import concourse.bass as bass
import concourse.mybir as mybir
from concourse.bass_utils import run_bass_kernel_spmd

F32 = mybir.dt.float32
I32 = mybir.dt.int32
BF16 = mybir.dt.bfloat16
AF = mybir.ActivationFunctionType
ALU = mybir.AluOpType
AX = mybir.AxisListType

S = 2048
H = 2048
E = 4096
NH = 64
PD = 64
NS = 64
CS = 128
NC = S // CS
CONV = E + 2 * NS
PROJ = E + CONV + NH
IFF = 5632
KCV = 4
NT = S // 128
ND = H // 128
NE = E // 128
NI = IFF // 128

C_LOG = float(1.0 / (2.0 * np.log(2.0)))
LN2 = float(np.log(2.0))


class Eng:

    def __init__(self, eng, sem, step, track=True):
        self.e = eng
        self.sem = sem
        self.step = step
        self.n = 0
        self.waited = {}
        self.track = track
        self.dw = set()
        self.dr = set()

    def wait(self, tok):
        if tok is None:
            return
        sem, val = tok
        if val <= 0:
            return
        if self.waited.get(id(sem), -1) >= val:
            return
        self.e.wait_ge(sem, val)
        self.waited[id(sem)] = val

    def pre(self, reads=(), writes=(), waits=()):
        if self.step == 16 and self.n > 0:
            self.wait((self.sem, self.n))
        for t in waits:
            self.wait(t)
        if self.track:
            if (any(k in self.dw for k in reads)
                    or any(k in self.dw for k in writes)
                    or any(k in self.dr for k in writes)):
                self.e.drain()
                self.dw.clear()
                self.dr.clear()
            self.dr.update(reads)
            self.dw.update(writes)

    def inc(self, instr):
        self.n += self.step
        instr.then_inc(self.sem, self.step)
        return (self.sem, self.n)

    def tok(self):
        return (self.sem, self.n)


def build_nc():
    nc = bass.Bass("TRN2", target_bir_lowering=False, debug=False)

    xin = nc.dram_tensor("x", [S, H], F32, kind="ExternalInput").ap()
    winT = nc.dram_tensor("winT", [H, PROJ], F32, kind="ExternalInput").ap()
    woutT = nc.dram_tensor("woutT", [E, H], F32, kind="ExternalInput").ap()
    wgN = nc.dram_tensor("wgN", [IFF, H], F32, kind="ExternalInput").ap()
    wuN = nc.dram_tensor("wuN", [IFF, H], F32, kind="ExternalInput").ap()
    wdN = nc.dram_tensor("wdN", [H, IFF], F32, kind="ExternalInput").ap()
    convw = nc.dram_tensor("convw", [CONV, KCV], F32, kind="ExternalInput").ap()
    convb = nc.dram_tensor("convb", [CONV, 1], F32, kind="ExternalInput").ap()
    dtb_r = nc.dram_tensor("dtb_r", [1, NH], F32, kind="ExternalInput").ap()
    dtb_c = nc.dram_tensor("dtb_c", [NH, 1], F32, kind="ExternalInput").ap()
    alog_r = nc.dram_tensor("alog_r", [1, NH], F32, kind="ExternalInput").ap()
    alog_c = nc.dram_tensor("alog_c", [NH, 1], F32, kind="ExternalInput").ap()
    d_r = nc.dram_tensor("d_r", [1, NH], F32, kind="ExternalInput").ap()
    nw_r = nc.dram_tensor("nw_r", [1, E], F32, kind="ExternalInput").ap()
    ident_i = nc.dram_tensor("ident", [128, 128], BF16, kind="ExternalInput").ap()
    identf_i = nc.dram_tensor("identf", [128, 128], F32, kind="ExternalInput").ap()
    cuml_i = nc.dram_tensor("cuml", [128, 128], F32, kind="ExternalInput").ap()
    ones_i = nc.dram_tensor("ones", [128, 128], F32, kind="ExternalInput").ap()
    ddelta_i = nc.dram_tensor("ddelta", [NH, NH * CS], F32, kind="ExternalInput").ap()

    out = nc.dram_tensor("out", [S, H], F32, kind="ExternalOutput").ap()

    qxT_scr = nc.dram_tensor("qxT_scr", [H, S], BF16, kind="Internal").ap()
    gate_scr = nc.dram_tensor("gate_scr", [S, E], BF16, kind="Internal").ap()
    hT_scr = nc.dram_tensor("hT_scr", [S, E], BF16, kind="Internal").ap()
    ynT_scr = nc.dram_tensor("ynT_scr", [E, S], BF16, kind="Internal").ap()
    T_scr = nc.dram_tensor("T_scr", [S, IFF], F32, kind="Internal").ap()
    qt_scr = nc.dram_tensor("qt_scr", [S, IFF], BF16, kind="Internal").ap()
    s_scr = nc.dram_tensor("s_scr", [S, H], BF16, kind="Internal").ap()

    st = contextlib.ExitStack()

    def mk(stack):
        def f(name, shape, dt=F32):
            return stack.enter_context(nc.sbuf_tensor(name, list(shape), dt))
        return f

    sb = mk(st)

    with st:
        ident = sb("identb", (128, 128), BF16)
        identf = sb("identfb", (128, 128), F32)
        cuml = sb("cumlb", (128, 128), F32)
        ones = sb("onesb", (128, 128), F32)
        sx = sb("sx", (128, NT), F32)
        r2acc = sb("r2acc", (128, NT), F32)
        r2sc = sb("r2sc", (128, NT), F32)
        r2q = sb("r2q", (128, NT), F32)
        DB = sb("DB", (128, NH), F32)
        dtbB = sb("dtbB", (128, NH), F32)
        negAB = sb("negAB", (128, NH), F32)
        negAc = sb("negAc", (NH, 1), F32)
        dtbC = sb("dtbC", (NH, 1), F32)
        nwB = sb("nwB", (128, E), BF16)
        rowbuf = sb("rowbuf", (1, E), F32)
        cbias = sb("cbias", (128, 6), F32)

        def psal(name, shape, dt=F32):
            return st.enter_context(nc.psum_tensor(name, list(shape), dt))
        bankA = psal("bankA", (128, 512))
        bankB = psal("bankB", (128, 512))
        bankC = psal("bankC", (128, 512))
        bankD = psal("bankD", (128, 512))
        bank7 = psal("bank7", (128, 512))
        bank8 = psal("bank8", (128, 512))
        ptrbA = psal("ptrbA", (128, 128), BF16)
        ptrbB = psal("ptrbB", (128, 128), BF16)
        pmm = [bankA, bankB]
        ptrb = [ptrbA, ptrbB]
        ptrf = bank7[:, 256:384]
        pbc = bank7[:, 384:512]
        pGm = bank7[:, 0:128]
        pacs = bank7[:, 128:192]
        pacs2 = bank7[:, 192:256]
        pdtF = bank8
        sem_ld = st.enter_context(nc.semaphore("s_ld"))
        sem_st = st.enter_context(nc.semaphore("s_st"))
        sem_t = st.enter_context(nc.semaphore("s_t"))
        sem_v = st.enter_context(nc.semaphore("s_v"))
        sem_s = st.enter_context(nc.semaphore("s_s"))

        sy = Eng(nc.sync, sem_ld, 16, track=False)
        gp = Eng(nc.gpsimd, sem_st, 16, track=False)
        te = Eng(nc.tensor, sem_t, 1, track=False)
        ve = Eng(nc.vector, sem_v, 1)
        se = Eng(nc.scalar, sem_s, 1)

        def barrier():
            return [te.tok(), ve.tok(), se.tok(), sy.tok(), gp.tok()]

        sy.pre()
        sy.inc(sy.e.dma_start(out=ident[:], in_=ident_i))
        sy.inc(sy.e.dma_start(out=identf[:], in_=identf_i))
        sy.inc(sy.e.dma_start(out=cuml[:], in_=cuml_i))
        sy.inc(sy.e.dma_start(out=ones[:], in_=ones_i))
        sy.inc(sy.e.dma_start(out=negAc[:], in_=alog_c))
        tld0 = sy.inc(sy.e.dma_start(out=dtbC[:], in_=dtb_c))

        ve.pre(writes=("cbias",))
        ve.e.memset(cbias[:, 0:1], 1e-5)
        ve.e.memset(cbias[:, 1:2], 1e-38)
        ve.e.memset(cbias[:, 2:3], float(-3.0 * LN2))
        ve.e.memset(cbias[:, 3:4], 1e-6)
        t_cbias = ve.inc(ve.e.memset(cbias[:, 4:5], float(3.0 * LN2)))

        se.pre(reads=("negAc",), writes=("negAc",), waits=[tld0])
        tse = se.inc(se.e.activation(out=negAc[:], in_=negAc[:], func=AF.Exp))
        ve.pre(reads=("negAc",), writes=("negAc",), waits=[tse])
        ve.inc(ve.e.tensor_scalar(out=negAc[:], in0=negAc[:],
                                  scalar1=-1.0, scalar2=None, op0=ALU.mult))

        def bcast_row(src_row, dst, n, pre_neg_exp=False):
            sy.pre(waits=[ve.tok(), te.tok()])
            t1 = sy.inc(sy.e.dma_start(out=rowbuf[0:1, 0:n], in_=src_row))
            if pre_neg_exp:
                se.pre(reads=("rowbuf",), writes=("rowbuf",), waits=[t1])
                t2 = se.inc(se.e.activation(out=rowbuf[0:1, 0:n],
                                            in_=rowbuf[0:1, 0:n], func=AF.Exp))
                ve.pre(reads=("rowbuf",), writes=("rowbuf",), waits=[t2])
                t1 = ve.inc(ve.e.tensor_scalar(out=rowbuf[0:1, 0:n], in0=rowbuf[0:1, 0:n],
                                               scalar1=-1.0, scalar2=None, op0=ALU.mult))
            tout = None
            for off in range(0, n, 128):
                w = min(128, n - off)
                te.pre(waits=[t1, ve.tok()])
                tm = te.inc(te.e.matmul(out=bank7[:, 384:384 + w], lhsT=ones[0:1, :],
                                        rhs=rowbuf[0:1, off:off + w], start=True, stop=True))
                ve.pre(writes=("bc_dst",), waits=[tm])
                tout = ve.inc(ve.e.tensor_copy(out=dst[:, off:off + w], in_=bank7[:, 384:384 + w]))
            return tout

        bcast_row(dtb_r, dtbB, NH)
        bcast_row(d_r, DB, NH)
        bcast_row(nw_r, nwB, E)
        bcast_row(alog_r, negAB, NH, pre_neg_exp=True)
        ve.pre(writes=("r2acc",))
        ve.e.memset(r2acc[:], 0.0)

        def fp8q(v_ap, out_ap, wa, wb, wc, wi, kv, ko, waits=()):
            se.pre(reads=(kv,), writes=("q_a",), waits=list(waits) + [t_cbias])
            se.e.activation(out=wa, in_=v_ap, func=AF.Square)
            se.pre(reads=("q_a",), writes=("q_b",))
            ts_ = se.inc(se.e.activation(out=wb, in_=wa, func=AF.Ln,
                                         bias=cbias[:, 1:2], scale=1.0))
            ve.pre(reads=("q_b",), writes=("q_a",), waits=[ts_])
            tv = ve.inc(ve.e.tensor_scalar(out=wa, in0=wb, scalar1=C_LOG, scalar2=-6.0,
                                           op0=ALU.mult, op1=ALU.max))
            se.pre(reads=("q_a",), writes=("q_i",), waits=[tv])
            se.e.activation(out=wi, in_=wa, func=AF.Copy)
            se.pre(reads=("q_i",), writes=("q_b",))
            ts_ = se.inc(se.e.activation(out=wb, in_=wi, func=AF.Copy))
            ve.pre(reads=("q_a", "q_b"), writes=("q_c",), waits=[ts_])
            ve.e.tensor_tensor(out=wc, in0=wb, in1=wa, op=ALU.is_gt)
            ve.pre(reads=("q_b", "q_c"), writes=("q_b",))
            tv = ve.inc(ve.e.tensor_tensor(out=wb, in0=wb, in1=wc, op=ALU.subtract))
            se.pre(reads=("q_b",), writes=("q_a",), waits=[tv])
            se.e.activation(out=wa, in_=wb, func=AF.Exp,
                            bias=cbias[:, 4:5], scale=float(-LN2))
            se.pre(reads=("q_b",), writes=("q_c",))
            ts_ = se.inc(se.e.activation(out=wc, in_=wb, func=AF.Exp,
                                         bias=cbias[:, 2:3], scale=float(LN2)))
            ve.pre(reads=("q_a", kv), writes=("q_b",), waits=[ts_])
            ve.e.tensor_tensor(out=wb, in0=v_ap, in1=wa, op=ALU.mult)
            ve.pre(reads=("q_b",), writes=("q_a",))
            tv = ve.inc(ve.e.tensor_scalar(out=wa, in0=wb, scalar1=0.5, scalar2=None,
                                           op0=ALU.add))
            se.pre(reads=("q_a",), writes=("q_i",), waits=[tv])
            ts_ = se.inc(se.e.activation(out=wi, in_=wa, func=AF.Copy))
            ve.pre(reads=("q_i",), writes=("q_b",), waits=[ts_])
            ve.e.tensor_copy(out=wb, in_=wi)
            ve.pre(reads=("q_a", "q_b"), writes=(kv,))
            ve.e.tensor_tensor(out=v_ap, in0=wb, in1=wa, op=ALU.is_gt)
            ve.pre(reads=(kv, "q_b"), writes=("q_b",))
            ve.e.tensor_tensor(out=wb, in0=wb, in1=v_ap, op=ALU.subtract)
            ve.pre(reads=("q_b", "q_c"), writes=(ko,))
            return ve.inc(ve.e.tensor_tensor(out=out_ap, in0=wb, in1=wc, op=ALU.mult))

        stBC = contextlib.ExitStack()
        sbBC = mk(stBC)
        dtsT = sbBC("dtsT", (128, NC * NH), F32)
        adtT = sbBC("adtT", (128, NC * NH), F32)
        adtF = sbBC("adtF", (NH, S), F32)
        BCf = sbBC("BCf", (128, S), BF16)
        BCt = sbBC("BCt", (128, S), BF16)
        Cf_t = sbBC("Cf_t", (NS, S), BF16)

        stX = contextlib.ExitStack()
        sbX = mk(stX)
        xnT = sbX("xnT", (128, ND * S), BF16)

        stA = contextlib.ExitStack()
        sbA = mk(stA)
        with stA:
            xb = [sbA("xb0", (128, H)), sbA("xb1", (128, H))]
            xnf = sbA("xnf", (128, H))
            waA = sbA("waA", (128, H))
            wbA = sbA("wbA", (128, H))
            wcA = sbA("wcA", (128, H))
            wiA = sbA("wiA", (128, H), I32)
            xnb = sbA("xnbA", (128, H), BF16)
            qxb = sbA("qxbA", (128, H), BF16)
            qxs = sbA("qxsA", (128, H), BF16)
            colA = sbA("colA", (128, 8))
            t_xuse = [None, None]
            t_tr = [None, None]
            t_qdma = None
            for tt in range(NT):
                sl = tt % 2
                kx = "xb%d" % sl
                sy.pre(waits=[t_xuse[sl]])
                t_ld = sy.inc(sy.e.dma_start(out=xb[sl][:],
                                             in_=xin[tt * 128:(tt + 1) * 128, :]))
                ve.pre(reads=(kx,), writes=("colA",), waits=[t_ld])
                ve.e.reduce_sum(out=colA[:, 0:1], in_=xb[sl][:], axis=AX.X)
                ve.pre(reads=("colA",), writes=("colA",))
                ve.e.tensor_scalar(out=colA[:, 1:2], in0=colA[:, 0:1],
                                   scalar1=float(-1.0 / H), scalar2=None, op0=ALU.mult)
                ve.pre(reads=(kx, "colA"), writes=(kx,))
                tv = ve.inc(ve.e.tensor_scalar(out=xb[sl][:], in0=xb[sl][:],
                                               scalar1=colA[:, 1:2], scalar2=None,
                                               op0=ALU.add))
                se.pre(reads=(kx,), writes=("waA", "colAs"), waits=[tv])
                se.e.activation(out=waA[:], in_=xb[sl][:], func=AF.Square,
                                scale=1.0, accum_out=colA[:, 2:3])
                se.pre(reads=("colAs",), writes=("colAs",))
                ts_ = se.inc(se.e.activation(out=colA[:, 3:4], in_=colA[:, 2:3],
                                             func=AF.Sqrt, bias=cbias[:, 0:1],
                                             scale=float(1.0 / H)))
                ve.pre(reads=("colAs",), writes=("colAs",), waits=[ts_])
                ve.e.reciprocal(out=colA[:, 3:4], in_=colA[:, 3:4])
                ve.pre(reads=(kx, "colAs"), writes=("xnf",))
                tv_xn = ve.inc(ve.e.tensor_scalar(out=xnf[:], in0=xb[sl][:],
                                                  scalar1=colA[:, 3:4], scalar2=None,
                                                  op0=ALU.mult))
                t_xuse[sl] = tv_xn
                ve.pre(reads=("xnf",), writes=("colA",))
                ve.e.reduce_max(out=colA[:, 4:5], in_=xnf[:], axis=AX.X,
                                apply_absolute_value=True)
                ve.pre(reads=("colA",), writes=("colA",))
                ve.e.tensor_scalar(out=colA[:, 5:6], in0=colA[:, 4:5],
                                   scalar1=1e-30, scalar2=None, op0=ALU.max)
                ve.pre(reads=("colA",), writes=("colA",))
                ve.e.reciprocal(out=colA[:, 5:6], in_=colA[:, 5:6])
                ve.pre(reads=("colA",), writes=("colA",))
                ve.e.tensor_scalar(out=colA[:, 5:6], in0=colA[:, 5:6],
                                   scalar1=448.0, scalar2=None, op0=ALU.mult)
                ve.pre(reads=("colA",), writes=("sx",))
                ve.e.tensor_scalar(out=sx[:, tt:tt + 1], in0=colA[:, 4:5],
                                   scalar1=float(1.0 / 448.0), scalar2=None, op0=ALU.mult)
                se.pre(reads=("xnf",), writes=("xnbA",), waits=[tv_xn])
                t_xnb = se.inc(se.e.activation(out=xnb[:], in_=xnf[:], func=AF.Copy))
                ve.pre(reads=("xnf", "colA"), writes=("xnf",), waits=[t_xnb])
                tvp = ve.inc(ve.e.tensor_scalar(out=xnf[:], in0=xnf[:], scalar1=colA[:, 5:6],
                                                scalar2=None, op0=ALU.mult))
                t_qx = fp8q(xnf[:], qxb[:], waA[:], wbA[:], wcA[:], wiA[:], "xnf", "qxbA",
                            waits=[tvp])
                for kt in range(ND):
                    s2 = kt % 2
                    te.pre(waits=[t_xnb, t_tr[s2]])
                    tm = te.inc(te.e.transpose(out=ptrb[s2][:],
                                               in_=xnb[:, kt * 128:(kt + 1) * 128],
                                               identity=ident[:]))
                    se.pre(writes=("xnT",), waits=[tm])
                    t_tr[s2] = se.inc(se.e.activation(func=AF.Copy, 
                        out=xnT[:, kt * S + tt * 128: kt * S + (tt + 1) * 128],
                        in_=ptrb[s2][:]))
                for kt in range(ND):
                    s2 = kt % 2
                    te.pre(waits=[t_qx, t_tr[s2], t_qdma])
                    tm = te.inc(te.e.transpose(out=ptrb[s2][:],
                                               in_=qxb[:, kt * 128:(kt + 1) * 128],
                                               identity=ident[:]))
                    se.pre(writes=("qxs",), waits=[tm])
                    t_tr[s2] = se.inc(se.e.activation(func=AF.Copy, out=qxs[:, kt * 128:(kt + 1) * 128],
                                                       in_=ptrb[s2][:]))
                    gp.pre(waits=[t_tr[s2]])
                    t_qdma = gp.inc(gp.e.dma_start(
                        out=qxT_scr[kt * 128:(kt + 1) * 128, tt * 128:(tt + 1) * 128],
                        in_=qxs[:, kt * 128:(kt + 1) * 128]))
        barA = barrier()

        stB = contextlib.ExitStack()
        sbB = mk(stB)
        with stB:
            wld = [sbB("wld0", (128, 4096)), sbB("wld1", (128, 4096))]
            wbf = [sbB("wbf0", (128, 4096), BF16), sbB("wbf1", (128, 4096), BF16)]
            convin = sbB("convin", (128, S))
            convC = sbB("convC", (128, S))
            convsil = sbB("convsil", (128, S), BF16)
            hstg = sbB("hstg", (128, S), BF16)
            gstg = [sbB("gstg0", (128, 256), BF16), sbB("gstg1", (128, 256), BF16)]
            dtw = sbB("dtw", (128, ND * NH), BF16)
            dtrawT = sbB("dtrawT", (128, NC * NH))
            dtrawF = sbB("dtrawF", (NH, S))
            tmpT = convC[:, 0:NC * NH]
            tmpF = convin[0:NH, 0:S]
            cw = sbB("cw", (128, 33 * KCV))
            cb = sbB("cb", (128, 33))

            sy.pre(waits=barA)
            for ct in range(33):
                sy.inc(sy.e.dma_start(out=cw[:, ct * KCV:(ct + 1) * KCV],
                                      in_=convw[ct * 128:(ct + 1) * 128, :]))
                sy.inc(sy.e.dma_start(out=cb[:, ct:ct + 1],
                                      in_=convb[ct * 128:(ct + 1) * 128, :]))
            t_dtw = None
            for kt in range(ND):
                t_dtw = sy.inc(sy.e.dma_start(out=wld[0][:, kt * NH:(kt + 1) * NH],
                                              in_=winT[kt * 128:(kt + 1) * 128,
                                                       E + CONV:PROJ]))
            t_cwall = sy.tok()
            se.pre(reads=("wld0",), writes=("dtw",), waits=barA + [t_dtw])
            t_dtwb = se.inc(se.e.activation(out=dtw[:], in_=wld[0][:, 0:ND * NH],
                                            func=AF.Copy))

            pdt = [bankC[:, 0:64], bankD[:, 0:64]]
            t_pdt = [None, None]
            for tt in range(NT):
                s2 = tt % 2
                te.pre(waits=barA + [t_dtwb, t_pdt[s2]])
                for kt in range(ND):
                    mmi = te.e.matmul(out=pdt[s2],
                                      lhsT=xnT[:, kt * S + tt * 128: kt * S + (tt + 1) * 128],
                                      rhs=dtw[:, kt * NH:(kt + 1) * NH],
                                      start=(kt == 0), stop=(kt == ND - 1))
                tmm = te.inc(mmi)
                ve.pre(writes=("dtrawT",), waits=[tmm])
                t_pdt[s2] = ve.inc(ve.e.tensor_copy(out=dtrawT[:, tt * NH:(tt + 1) * NH],
                                                    in_=pdt[s2]))
            t_pdtF = None
            for tb in range(8):
                te.pre(waits=[t_dtwb, t_pdtF, t_pdt[0], t_pdt[1]])
                for kt in range(ND):
                    mmi = te.e.matmul(out=pdtF[0:64, 0:256],
                                      lhsT=dtw[:, kt * NH:(kt + 1) * NH],
                                      rhs=xnT[:, kt * S + tb * 256: kt * S + (tb + 1) * 256],
                                      start=(kt == 0), stop=(kt == ND - 1))
                tmm = te.inc(mmi)
                ve.pre(writes=("dtrawF",), waits=[tmm])
                t_pdtF = ve.inc(ve.e.tensor_copy(out=dtrawF[:, tb * 256:(tb + 1) * 256],
                                                 in_=pdtF[0:64, 0:256]))

            for c in range(NC):
                ve.pre(reads=("dtrawT", "dtbB"), writes=("dtrawT",))
                ve.e.tensor_tensor(out=dtrawT[:, c * NH:(c + 1) * NH],
                                   in0=dtrawT[:, c * NH:(c + 1) * NH], in1=dtbB[:], op=ALU.add)
            ve.pre(reads=("dtrawT",))
            tv = ve.inc(ve.e.tensor_copy(out=tmpT, in_=dtrawT[:]))
            se.pre(reads=("convC",), writes=("convC",), waits=[tv])
            ts_ = se.inc(se.e.activation(out=tmpT, in_=dtrawT[:], func=AF.Exp))
            ve.pre(reads=("convC",), writes=("convC",), waits=[ts_])
            tv = ve.inc(ve.e.tensor_scalar(out=tmpT, in0=tmpT, scalar1=1.0,
                                           scalar2=None, op0=ALU.add))
            se.pre(reads=("convC",), writes=("dtsT",), waits=[tv])
            t_dtsT = se.inc(se.e.activation(out=dtsT[:], in_=tmpT, func=AF.Ln))
            for c in range(NC):
                ve.pre(reads=("dtsT", "negAB"), writes=("adtT",), waits=[t_dtsT])
                ve.e.tensor_tensor(out=adtT[:, c * NH:(c + 1) * NH],
                                   in0=dtsT[:, c * NH:(c + 1) * NH], in1=negAB[:], op=ALU.mult)
            t_adtT = ve.inc(ve.e.tensor_copy(out=adtT[:, 0:1], in_=adtT[:, 0:1]))

            ve.pre(reads=("dtrawF", "dtbC"), writes=("dtrawF",))
            tv = ve.inc(ve.e.tensor_scalar(out=dtrawF[:], in0=dtrawF[:],
                                           scalar1=dtbC[:, 0:1], scalar2=None, op0=ALU.add))
            se.pre(reads=("dtrawF",), writes=("convin",), waits=[tv])
            ts_ = se.inc(se.e.activation(out=tmpF, in_=dtrawF[:], func=AF.Exp))
            ve.pre(reads=("convin",), writes=("convin",), waits=[ts_])
            tv = ve.inc(ve.e.tensor_scalar(out=tmpF, in0=tmpF, scalar1=1.0,
                                           scalar2=None, op0=ALU.add))
            se.pre(reads=("convin",), writes=("dtrawF",), waits=[tv])
            ts_ = se.inc(se.e.activation(out=dtrawF[:], in_=tmpF, func=AF.Ln))
            ve.pre(reads=("dtrawF", "negAc"), writes=("adtF",), waits=[ts_])
            t_adtF = ve.inc(ve.e.tensor_scalar(out=adtF[:], in0=dtrawF[:],
                                               scalar1=negAc[:, 0:1], scalar2=None,
                                               op0=ALU.mult))

            t_wld = [None, None]
            t_wbf = [t_dtwb, None]
            t_pmm = [None, None]
            t_hdma = None
            t_convsil_used = None
            for ct in range(33):
                sl = ct % 2
                sy.pre(waits=[t_wbf[sl]])
                t_w = None
                for kt in range(ND):
                    t_w = sy.inc(sy.e.dma_start(
                        out=wld[sl][:, kt * 128:(kt + 1) * 128],
                        in_=winT[kt * 128:(kt + 1) * 128,
                                 E + ct * 128:E + (ct + 1) * 128]))
                se.pre(reads=("wld%d" % sl,), writes=("wbf%d" % sl,), waits=[t_w, t_pmm[sl]])
                t_wbf[sl] = se.inc(se.e.activation(out=wbf[sl][:, 0:2048],
                                                   in_=wld[sl][:, 0:2048], func=AF.Copy))
                t_cp = None
                for tb in range(8):
                    p2 = tb % 2
                    te.pre(waits=[t_wbf[sl], t_pmm[p2]])
                    for kt in range(ND):
                        mmi = te.e.matmul(out=pmm[p2][:, 0:256],
                                          lhsT=wbf[sl][:, kt * 128:(kt + 1) * 128],
                                          rhs=xnT[:, kt * S + tb * 256: kt * S + (tb + 1) * 256],
                                          start=(kt == 0), stop=(kt == ND - 1))
                    tmm = te.inc(mmi)
                    ve.pre(writes=("convin",), waits=[tmm])
                    t_cp = ve.inc(ve.e.tensor_copy(out=convin[:, tb * 256:(tb + 1) * 256],
                                                   in_=pmm[p2][:, 0:256]))
                    t_pmm[p2] = t_cp
                ve.pre(reads=("convin", "cwb"), writes=("convC",), waits=[t_cp, t_cwall])
                ve.e.tensor_scalar(out=convC[:], in0=convin[:],
                                   scalar1=cw[:, ct * KCV + 3:ct * KCV + 4],
                                   scalar2=cb[:, ct:ct + 1], op0=ALU.mult, op1=ALU.add)
                for k in (2, 1, 0):
                    sh = 3 - k
                    ve.pre(reads=("convin", "convC"), writes=("convC",))
                    ve.e.scalar_tensor_tensor(out=convC[:, sh:S], in0=convin[:, 0:S - sh],
                                              scalar=cw[:, ct * KCV + k:ct * KCV + k + 1],
                                              in1=convC[:, sh:S], op0=ALU.mult, op1=ALU.add)
                tv_conv = ve.inc(ve.e.tensor_copy(out=convC[:, 0:1], in_=convC[:, 0:1]))
                if ct < 32:
                    se.pre(reads=("convC",), writes=("convsil",),
                           waits=[tv_conv, t_convsil_used])
                    t_sil = se.inc(se.e.activation(out=convsil[:], in_=convC[:], func=AF.Silu))
                    for j in range(NC):
                        s2 = j % 2
                        te.pre(waits=[t_sil, t_tr[s2]])
                        tm = te.inc(te.e.transpose(out=ptrb[s2][:],
                                                   in_=convsil[:, j * 128:(j + 1) * 128],
                                                   identity=ident[:]))
                        se.pre(writes=("hstg",), waits=[tm, t_hdma])
                        t_tr[s2] = se.inc(se.e.activation(func=AF.Copy, 
                            out=hstg[:, j * 128:(j + 1) * 128],
                            in_=ptrb[s2][:]))
                        gp.pre(waits=[t_tr[s2]])
                        t_hdma = gp.inc(gp.e.dma_start(
                            out=hT_scr[j * 128:(j + 1) * 128, ct * 128:(ct + 1) * 128],
                            in_=hstg[:, j * 128:(j + 1) * 128]))
                    t_convsil_used = t_tr[(NC - 1) % 2]
                else:
                    se.pre(reads=("convC",), writes=("BCf",), waits=[tv_conv])
                    t_sil = se.inc(se.e.activation(out=BCf[:], in_=convC[:], func=AF.Silu))
                    sy.pre(waits=[t_sil])
                    sy.inc(sy.e.dma_start(out=Cf_t[:], in_=BCf[64:128, :]))
                    for j in range(NC):
                        s2 = j % 2
                        te.pre(waits=[t_sil, t_tr[s2]])
                        tm = te.inc(te.e.transpose(out=ptrb[s2][:],
                                                   in_=BCf[:, j * 128:(j + 1) * 128],
                                                   identity=ident[:]))
                        se.pre(writes=("BCt",), waits=[tm])
                        t_tr[s2] = se.inc(se.e.activation(func=AF.Copy, 
                            out=BCt[:, j * 128:(j + 1) * 128],
                            in_=ptrb[s2][:]))

            t_gdma = [None, None]
            for gi in range(16):
                sl = gi % 2
                sy.pre(waits=[t_wbf[sl]])
                t_w = None
                for kt in range(ND):
                    t_w = sy.inc(sy.e.dma_start(
                        out=wld[sl][:, kt * 256:(kt + 1) * 256],
                        in_=winT[kt * 128:(kt + 1) * 128, gi * 256:(gi + 1) * 256]))
                se.pre(reads=("wld%d" % sl,), writes=("wbf%d" % sl,),
                       waits=[t_w, t_pmm[0], t_pmm[1]])
                t_wbf[sl] = se.inc(se.e.activation(out=wbf[sl][:], in_=wld[sl][:],
                                                   func=AF.Copy))
                for tt in range(NT):
                    p2 = tt % 2
                    te.pre(waits=[t_wbf[sl], t_pmm[p2]])
                    for kt in range(ND):
                        mmi = te.e.matmul(out=pmm[p2][:, 0:256],
                                          lhsT=xnT[:, kt * S + tt * 128: kt * S + (tt + 1) * 128],
                                          rhs=wbf[sl][:, kt * 256:(kt + 1) * 256],
                                          start=(kt == 0), stop=(kt == ND - 1))
                    tmm = te.inc(mmi)
                    se.pre(writes=("gstg%d" % p2,), waits=[tmm, t_gdma[p2]])
                    t_sil = se.inc(se.e.activation(out=gstg[p2][:], in_=pmm[p2][:, 0:256],
                                                   func=AF.Silu))
                    t_pmm[p2] = t_sil
                    gp.pre(waits=[t_sil])
                    t_gdma[p2] = gp.inc(gp.e.dma_start(
                        out=gate_scr[tt * 128:(tt + 1) * 128, gi * 256:(gi + 1) * 256],
                        in_=gstg[p2][:]))
        stX.close()
        barB = barrier()

        stB2 = contextlib.ExitStack()
        sbB2 = mk(stB2)
        with stB2:
            qxTb = sbB2("qxTb", (128, ND * S), BF16)
            nsl = [sbB2("nsl0", (128, H)), sbB2("nsl1", (128, H))]
            wa2 = sbB2("wa2", (128, H))
            wb2 = sbB2("wb2", (128, H))
            wc2 = sbB2("wc2", (128, H))
            wi2 = sbB2("wi2", (128, H), I32)
            qnat = sbB2("qnat", (128, H), BF16)
            wbfG = sbB2("wbfG", (128, 4096), BF16)
            wbfU = sbB2("wbfU", (128, 4096), BF16)
            dqB = sbB2("dqB", (128, 512))
            dqrow = sbB2("dqrow", (1, 128))
            Tg = sbB2("Tg", (128, 256))
            Tsg = sbB2("Tsg", (128, 256))
            Tu = sbB2("Tu", (128, 256))
            Tbuf = sbB2("TbufB2", (128, 256))
            colB = sbB2("colB", (128, 8))

            sy.pre(waits=barB)
            t_qxTb = None
            for kt in range(ND):
                t_qxTb = sy.inc(sy.e.dma_start(out=qxTb[:, kt * S:(kt + 1) * S],
                                               in_=qxT_scr[kt * 128:(kt + 1) * 128, :]))

            t_nsl = [None, None]
            t_wbfGU = None
            t_Tdma = None
            for ib in range(22):
                for m in range(2):
                    wsrc = wgN if m == 0 else wuN
                    wdst = wbfG if m == 0 else wbfU
                    for isl in range(2):
                        iabs = ib * 2 + isl
                        sl = iabs % 2
                        sy.pre(waits=[t_nsl[sl]] + (barB if ib == 0 else []))
                        t_w = sy.inc(sy.e.dma_start(out=nsl[sl][:],
                                                    in_=wsrc[iabs * 128:(iabs + 1) * 128, :]))
                        kn = "nsl%d" % sl
                        ve.pre(reads=(kn,), writes=("colB",), waits=[t_w])
                        ve.e.reduce_max(out=colB[:, 0:1], in_=nsl[sl][:], axis=AX.X,
                                        apply_absolute_value=True)
                        ve.pre(reads=("colB",), writes=("colB",))
                        ve.e.tensor_scalar(out=colB[:, 1:2], in0=colB[:, 0:1],
                                           scalar1=1e-30, scalar2=None, op0=ALU.max)
                        ve.pre(reads=("colB",), writes=("colB",))
                        ve.e.reciprocal(out=colB[:, 1:2], in_=colB[:, 1:2])
                        ve.pre(reads=("colB",), writes=("colB",))
                        ve.e.tensor_scalar(out=colB[:, 1:2], in0=colB[:, 1:2],
                                           scalar1=448.0, scalar2=None, op0=ALU.mult)
                        ve.pre(reads=("colB",), writes=("colB",))
                        tv_dq = ve.inc(ve.e.tensor_scalar(out=colB[:, 2:3], in0=colB[:, 0:1],
                                                          scalar1=float(1.0 / 448.0),
                                                          scalar2=None, op0=ALU.mult))
                        ve.pre(reads=(kn, "colB"), writes=(kn,))
                        tvp = ve.inc(ve.e.tensor_scalar(out=nsl[sl][:], in0=nsl[sl][:],
                                                        scalar1=colB[:, 1:2],
                                                        scalar2=None, op0=ALU.mult))
                        t_q = fp8q(nsl[sl][:], qnat[:], wa2[:], wb2[:], wc2[:], wi2[:],
                                   kn, "qnat", waits=[tvp])
                        t_nsl[sl] = t_q
                        te.pre(waits=[tv_dq])
                        tm = te.inc(te.e.transpose(out=ptrf[0:1, 0:128], in_=colB[:, 2:3],
                                                   identity=identf[:]))
                        ve.pre(writes=("dqrow",), waits=[tm])
                        tv = ve.inc(ve.e.tensor_copy(out=dqrow[:], in_=ptrf[0:1, 0:128]))
                        te.pre(waits=[tv, t_wbfGU])
                        tm = te.inc(te.e.matmul(out=pbc, lhsT=ones[0:1, :],
                                                rhs=dqrow[:], start=True, stop=True))
                        ve.pre(writes=("dqB",), waits=[tm])
                        ve.inc(ve.e.tensor_copy(
                            out=dqB[:, m * 256 + isl * 128: m * 256 + (isl + 1) * 128],
                            in_=pbc))
                        for kt in range(ND):
                            s2 = kt % 2
                            te.pre(waits=[t_q, t_tr[s2], t_wbfGU])
                            tm = te.inc(te.e.transpose(out=ptrb[s2][:],
                                                       in_=qnat[:, kt * 128:(kt + 1) * 128],
                                                       identity=ident[:]))
                            se.pre(writes=("wbfGU",), waits=[tm])
                            t_tr[s2] = se.inc(se.e.activation(func=AF.Copy, 
                                out=wdst[:, kt * 256 + isl * 128: kt * 256 + (isl + 1) * 128],
                                in_=ptrb[s2][:]))
                t_wstrips = se.tok()
                t_dqB = ve.tok()
                for tt in range(NT):
                    te.pre(waits=[t_wstrips, t_qxTb, t_pmm[0]])
                    for kt in range(ND):
                        mmi = te.e.matmul(out=pmm[0][:, 0:256],
                                          lhsT=qxTb[:, kt * S + tt * 128: kt * S + (tt + 1) * 128],
                                          rhs=wbfG[:, kt * 256:(kt + 1) * 256],
                                          start=(kt == 0), stop=(kt == ND - 1))
                    tG = te.inc(mmi)
                    te.pre(waits=[t_pmm[1]])
                    for kt in range(ND):
                        mmi = te.e.matmul(out=pmm[1][:, 0:256],
                                          lhsT=qxTb[:, kt * S + tt * 128: kt * S + (tt + 1) * 128],
                                          rhs=wbfU[:, kt * 256:(kt + 1) * 256],
                                          start=(kt == 0), stop=(kt == ND - 1))
                    tU = te.inc(mmi)
                    ve.pre(reads=("dqB", "sx"), writes=("Tg",), waits=[tG, t_dqB])
                    tv = ve.inc(ve.e.scalar_tensor_tensor(out=Tg[:], in0=pmm[0][:, 0:256],
                                                          scalar=sx[:, tt:tt + 1],
                                                          in1=dqB[:, 0:256],
                                                          op0=ALU.mult, op1=ALU.mult))
                    t_pmm[0] = tv
                    se.pre(reads=("Tg",), writes=("Tsg",), waits=[tv])
                    tsg = se.inc(se.e.activation(out=Tsg[:], in_=Tg[:], func=AF.Silu))
                    ve.pre(reads=("dqB", "sx"), writes=("Tu",), waits=[tU])
                    tv = ve.inc(ve.e.scalar_tensor_tensor(out=Tu[:], in0=pmm[1][:, 0:256],
                                                          scalar=sx[:, tt:tt + 1],
                                                          in1=dqB[:, 256:512],
                                                          op0=ALU.mult, op1=ALU.mult))
                    t_pmm[1] = tv
                    ve.pre(reads=("Tsg", "Tu"), writes=("TbufB2",), waits=[tsg, t_Tdma])
                    tv = ve.inc(ve.e.tensor_tensor(out=Tbuf[:], in0=Tsg[:], in1=Tu[:],
                                                   op=ALU.mult))
                    ve.pre(reads=("TbufB2",), writes=("colB",))
                    ve.e.reduce_max(out=colB[:, 3:4], in_=Tbuf[:], axis=AX.X,
                                    apply_absolute_value=True)
                    ve.pre(reads=("colB", "r2acc"), writes=("r2acc",))
                    ve.e.tensor_tensor(out=r2acc[:, tt:tt + 1], in0=r2acc[:, tt:tt + 1],
                                       in1=colB[:, 3:4], op=ALU.max)
                    gp.pre(waits=[tv])
                    t_Tdma = gp.inc(gp.e.dma_start(
                        out=T_scr[tt * 128:(tt + 1) * 128, ib * 256:(ib + 1) * 256],
                        in_=Tbuf[:]))
            ve.pre(reads=("r2acc",), writes=("r2sc",))
            ve.e.tensor_scalar(out=r2sc[:], in0=r2acc[:], scalar1=float(1.0 / 448.0),
                               scalar2=None, op0=ALU.mult)
            ve.pre(reads=("r2acc",), writes=("r2q",))
            ve.e.tensor_scalar(out=r2q[:], in0=r2acc[:], scalar1=1e-30,
                               scalar2=None, op0=ALU.max)
            ve.pre(reads=("r2q",), writes=("r2q",))
            ve.e.reciprocal(out=r2q[:], in_=r2q[:])
            ve.pre(reads=("r2q",), writes=("r2q",))
            ve.inc(ve.e.tensor_scalar(out=r2q[:], in0=r2q[:], scalar1=448.0,
                                      scalar2=None, op0=ALU.mult))
        barB2 = barrier()

        stC = contextlib.ExitStack()
        sbC = mk(stC)
        with stC:
            lnT = sbC("lnT", (128, NC * NH))
            acsT = sbC("acsT", (128, NC * NH))
            acsSB = sbC("acsSB", (128, NC * NH))
            eacsT = sbC("eacsT", (128, NC * NH))
            eacslast = sbC("eacslast", (128, NC * NH))
            wdecT = sbC("wdecT", (128, NC * NH))
            acsF = sbC("acsF", (NH, S))
            ddlhsT = sbC("ddlhsT", (65, 128))
            ddrhs = sbC("ddrhs", (65, NH * CS))
            Sst = sbC("Sst", (NH, NH * PD))
            hTt = [sbC("hTt0", (128, E), BF16), sbC("hTt1", (128, E), BF16)]
            gateS = [sbC("gateS0", (128, E), BF16), sbC("gateS1", (128, E), BF16)]
            Ychunk = sbC("Ychunk", (128, E))
            GmM = sbC("GmM", (128, 128), BF16)
            Mp = [sbC("Mp0", (128, 128), BF16), sbC("Mp1", (128, 128), BF16)]
            edd = [sbC("edd0", (128, 128)), sbC("edd1", (128, 128))]
            ddc = [sbC("ddc0", (128, 512)), sbC("ddc1", (128, 512))]
            SstB = [sbC("SstB0", (NH, PD), BF16), sbC("SstB1", (NH, PD), BF16)]
            BdecT = [sbC("BdecT0", (128, NS), BF16), sbC("BdecT1", (128, NS), BF16)]
            Ym = [sbC("Ym0", (128, PD)), sbC("Ym1", (128, PD))]
            ynb = sbC("ynb", (128, E), BF16)
            ytstg = [sbC("yts0", (128, 128), BF16), sbC("yts1", (128, 128), BF16)]
            colC = sbC("colC", (128, 12))

            se.pre(reads=("dtsT",), writes=("lnT",), waits=barB2)
            t_lnT = se.inc(se.e.activation(out=lnT[:], in_=dtsT[:], func=AF.Ln))
            t_acs = None
            for c in range(NC):
                te.pre(waits=barB2 + [t_acs])
                tm = te.inc(te.e.matmul(out=pacs, lhsT=cuml[:],
                                        rhs=adtT[:, c * NH:(c + 1) * NH],
                                        start=True, stop=True))
                ve.pre(writes=("acsT",), waits=[tm])
                t_acs = ve.inc(ve.e.tensor_copy(out=acsT[:, c * NH:(c + 1) * NH],
                                                in_=pacs))
                te.pre(waits=[t_acs])
                tm = te.inc(te.e.matmul(out=pacs2, lhsT=ones[:],
                                        rhs=adtT[:, c * NH:(c + 1) * NH],
                                        start=True, stop=True))
                ve.pre(writes=("acsSB",), waits=[tm])
                t_acs = ve.inc(ve.e.tensor_copy(out=acsSB[:, c * NH:(c + 1) * NH],
                                                in_=pacs2))
            se.pre(reads=("acsSB",), writes=("eacslast",), waits=[t_acs])
            se.inc(se.e.activation(out=eacslast[:], in_=acsSB[:], func=AF.Exp))
            se.pre(reads=("acsT",), writes=("eacsT",))
            t_eacsT = se.inc(se.e.activation(out=eacsT[:], in_=acsT[:], func=AF.Exp))
            ve.pre(reads=("acsSB", "acsT"), writes=("wdecT",), waits=[t_acs])
            tv = ve.inc(ve.e.tensor_tensor(out=wdecT[:], in0=acsSB[:], in1=acsT[:],
                                           op=ALU.subtract))
            se.pre(reads=("wdecT",), writes=("wdecT",), waits=[tv])
            ts_ = se.inc(se.e.activation(out=wdecT[:], in_=wdecT[:], func=AF.Exp))
            ve.pre(reads=("wdecT", "dtsT"), writes=("wdecT",), waits=[ts_])
            t_wdec = ve.inc(ve.e.tensor_tensor(out=wdecT[:], in0=wdecT[:], in1=dtsT[:],
                                               op=ALU.mult))
            t_acsF = None
            for c in range(NC):
                ve.pre(reads=("adtF",), writes=("acsF",))
                t_acsF = ve.inc(ve.e.tensor_tensor_scan(
                    out=acsF[:, c * CS:(c + 1) * CS], data0=adtF[:, c * CS:(c + 1) * CS],
                    data1=adtF[:, c * CS:(c + 1) * CS], initial=0.0,
                    op0=ALU.add, op1=ALU.bypass))
            ve.pre(writes=("Sst", "ddlhsT"))
            ve.e.memset(Sst[:], 0.0)
            ve.e.memset(ddlhsT[64:65, :], 1.0)
            tv_init = ve.inc(ve.e.memset(colC[:], 0.0))
            sy.pre(waits=barB2)
            t_delta = sy.inc(sy.e.dma_start(out=ddrhs[0:64, :], in_=ddelta_i))

            pYd = [bankA[:, 0:64], bankB[:, 0:64]]
            pYo = [bankA[:, 64:128], bankB[:, 64:128]]
            pZ = [bankA[0:64, 128:192], bankB[0:64, 128:192]]
            pdd2 = [bankC, bankD]
            t_hld = [None, None]
            t_gld = [None, None]
            t_ddrow = None
            t_ddmm = None
            t_lhs = None
            t_pdd = [None, None]
            t_pY = [None, None]
            t_pGm = None
            t_ych = None
            t_ydma = None
            t_sdma_out = None
            t_Mp = [None, None]
            t_edd = [None, None]
            t_SstB = [None, None]
            t_Bdec = [None, None]
            t_Sst_upd = None
            for c in range(NC):
                sl = c % 2
                sy.pre(waits=[t_hld[sl]] + barB2)
                t_h = sy.inc(sy.e.dma_start(out=hTt[sl][:],
                                            in_=hT_scr[c * 128:(c + 1) * 128, :]))
                sy.pre(waits=[t_gld[sl], gp.tok() if c == 0 else None])
                t_g = sy.inc(sy.e.dma_start(out=gateS[sl][:],
                                            in_=gate_scr[c * 128:(c + 1) * 128, :]))
                sy.pre(waits=[t_acsF, t_ddmm])
                t_row = sy.inc(sy.e.dma_start(out=ddrhs[64:65, :],
                                              in_=acsF[:, c * CS:(c + 1) * CS]))
                ve.pre(reads=("acsF",), writes=("ddlhsT",), waits=[t_ddmm, t_acsF, tv_init])
                t_lhs = ve.inc(ve.e.tensor_scalar(out=ddlhsT[0:64, :],
                                                  in0=acsF[:, c * CS:(c + 1) * CS],
                                                  scalar1=-1.0, scalar2=None, op0=ALU.mult))
                te.pre(waits=[t_pGm, t_delta, t_acs])
                tm = te.inc(te.e.matmul(out=pGm,
                                        lhsT=BCf[0:64, c * 128:(c + 1) * 128],
                                        rhs=Cf_t[:, c * 128:(c + 1) * 128],
                                        start=True, stop=True))
                ve.pre(reads=("cuml",), writes=("GmM",), waits=[tm, t_Mp[0], t_Mp[1]])
                t_pGm = ve.inc(ve.e.tensor_tensor(out=GmM[:], in0=pGm,
                                                  in1=cuml[:], op=ALU.mult))
                t_GmM = t_pGm
                t_ddmm_new = None
                for j in range(16):
                    p2 = j % 2
                    te.pre(waits=[t_row, t_lhs, t_pdd[p2]])
                    t_ddmm_new = te.inc(te.e.matmul(
                        out=pdd2[p2][:, 0:512], lhsT=ddlhsT[:],
                        rhs=ddrhs[:, j * 512:(j + 1) * 512], start=True, stop=True))
                    kdc = "ddc%d" % p2
                    ve.pre(writes=(kdc,), waits=[t_ddmm_new, t_edd[0], t_edd[1]])
                    t_pdd[p2] = ve.inc(ve.e.tensor_scalar(out=ddc[p2][:], in0=pdd2[p2][:, 0:512],
                                                          scalar1=0.0, scalar2=None,
                                                          op0=ALU.min))
                    for jj in range(4):
                        h = j * 4 + jj
                        h2 = h % 2
                        se.pre(reads=(kdc, "lnT"), writes=("edd%d" % h2,),
                               waits=[t_pdd[p2], t_lnT, t_Mp[h2]])
                        t_edd[h2] = se.inc(se.e.activation(
                            out=edd[h2][:], in_=ddc[p2][:, jj * 128:(jj + 1) * 128],
                            func=AF.Exp, bias=lnT[:, c * NH + h:c * NH + h + 1], scale=1.0))
                        ve.pre(reads=("edd%d" % h2, "GmM"), writes=("Mp%d" % h2,),
                               waits=[t_edd[h2], t_GmM, t_pY[h2]])
                        t_Mp[h2] = ve.inc(ve.e.tensor_tensor(out=Mp[h2][:], in0=edd[h2][:],
                                                             in1=GmM[:], op=ALU.mult))
                        se.pre(reads=("Sst",), writes=("SstB%d" % h2,),
                               waits=[t_Sst_upd, t_pY[h2]])
                        t_SstB[h2] = se.inc(se.e.activation(
                            out=SstB[h2][:], in_=Sst[0:64, h * PD:(h + 1) * PD],
                            func=AF.Copy))
                        ve.pre(reads=("BCt", "wdecT"), writes=("Bdec%d" % h2,),
                               waits=[t_wdec, t_pY[h2]])
                        t_Bdec[h2] = ve.inc(ve.e.tensor_scalar(
                            out=BdecT[h2][:], in0=BCt[:, c * 128:c * 128 + 64],
                            scalar1=wdecT[:, c * NH + h:c * NH + h + 1],
                            scalar2=None, op0=ALU.mult))
                        te.pre(waits=[t_Mp[h2], t_SstB[h2], t_Bdec[h2], t_h, t_pY[h2]])
                        te.e.matmul(out=pYd[h2], lhsT=Mp[h2][:],
                                    rhs=hTt[sl][:, h * PD:(h + 1) * PD],
                                    start=True, stop=True)
                        te.e.matmul(out=pYo[h2], lhsT=Cf_t[:, c * 128:(c + 1) * 128],
                                    rhs=SstB[h2][:], start=True, stop=True)
                        t_Ymm = te.inc(te.e.matmul(out=pZ[h2], lhsT=BdecT[h2][:],
                                                   rhs=hTt[sl][:, h * PD:(h + 1) * PD],
                                                   start=True, stop=True))
                        ve.pre(reads=("eacsT",), writes=("Ym%d" % h2,),
                               waits=[t_Ymm, t_eacsT])
                        ve.e.tensor_scalar(
                            out=Ym[h2][:], in0=pYo[h2],
                            scalar1=eacsT[:, c * NH + h:c * NH + h + 1],
                            scalar2=None, op0=ALU.mult)
                        ve.pre(reads=("Ym%d" % h2,), writes=("Ym%d" % h2,))
                        ve.e.tensor_tensor(out=Ym[h2][:], in0=Ym[h2][:],
                                           in1=pYd[h2], op=ALU.add)
                        ve.pre(reads=("Ym%d" % h2, "DB"), writes=("Ychunk",),
                               waits=[t_ych if h == 0 else None])
                        ve.e.scalar_tensor_tensor(
                            out=Ychunk[:, h * PD:(h + 1) * PD],
                            in0=hTt[sl][:, h * PD:(h + 1) * PD],
                            scalar=DB[:, h:h + 1], in1=Ym[h2][:],
                            op0=ALU.mult, op1=ALU.add)
                        ve.pre(reads=("Sst", "eacslast"), writes=("Sst",))
                        t_pY[h2] = ve.inc(ve.e.scalar_tensor_tensor(
                            out=Sst[0:64, h * PD:(h + 1) * PD],
                            in0=Sst[0:64, h * PD:(h + 1) * PD],
                            scalar=eacslast[0:64, c * NH + h:c * NH + h + 1],
                            in1=pZ[h2], op0=ALU.mult, op1=ALU.add))
                        t_Sst_upd = t_pY[h2]
                t_ddmm = t_ddmm_new
                ve.pre(reads=("Ychunk",), writes=("Ychunk",), waits=[t_g, t_pY[0], t_pY[1]])
                tv = ve.inc(ve.e.tensor_tensor(out=Ychunk[:], in0=Ychunk[:],
                                               in1=gateS[sl][:], op=ALU.mult))
                t_gld[sl] = tv
                for q8 in range(8):
                    se.pre(reads=("Ychunk",), writes=("ddc0", "colC"), waits=[tv, t_pdd[0]])
                    se.e.activation(out=ddc[0][:], in_=Ychunk[:, q8 * 512:(q8 + 1) * 512],
                                    func=AF.Square, scale=1.0,
                                    accum_out=colC[:, q8:q8 + 1])
                se.pre(reads=("colC",))
                tsq = se.inc(se.e.activation(out=colC[:, 8:9], in_=colC[:, 0:1],
                                             func=AF.Copy))
                ve.pre(reads=("colC",), writes=("colC",), waits=[tsq])
                tv = ve.inc(ve.e.reduce_sum(out=colC[:, 9:10], in_=colC[:, 0:8],
                                            axis=AX.X))
                se.pre(reads=("colC",), writes=("colC",), waits=[tv])
                trs = se.inc(se.e.activation(out=colC[:, 11:12], in_=colC[:, 9:10],
                                             func=AF.Sqrt, bias=cbias[:, 3:4],
                                             scale=float(1.0 / E)))
                ve.pre(reads=("colC",), writes=("colC",), waits=[trs])
                ve.e.reciprocal(out=colC[:, 11:12], in_=colC[:, 11:12])
                ve.pre(reads=("Ychunk", "colC", "nwB"), writes=("ynb",),
                       waits=[t_ydma])
                t_yn = ve.inc(ve.e.scalar_tensor_tensor(out=ynb[:], in0=Ychunk[:],
                                                        scalar=colC[:, 11:12], in1=nwB[:],
                                                        op0=ALU.mult, op1=ALU.mult))
                t_ych = t_yn
                t_ytr = [None, None]
                for et in range(NE):
                    s2 = et % 2
                    te.pre(waits=[t_yn, t_ytr[s2]])
                    tm = te.inc(te.e.transpose(out=ptrb[s2][:],
                                               in_=ynb[:, et * 128:(et + 1) * 128],
                                               identity=ident[:]))
                    se.pre(writes=("yts%d" % s2,), waits=[tm, t_ydma])
                    tcp = se.inc(se.e.activation(func=AF.Copy, out=ytstg[s2][:],
                                                  in_=ptrb[s2][:]))
                    t_ytr[s2] = tcp
                    gp.pre(waits=[tcp])
                    t_ydma = gp.inc(gp.e.dma_start(
                        out=ynT_scr[et * 128:(et + 1) * 128, c * 128:(c + 1) * 128],
                        in_=ytstg[s2][:]))
                t_hld[sl] = t_pY[1]
        stBC.close()
        barC = barrier()

        stD = contextlib.ExitStack()
        sbD = mk(stD)
        pdd2D = [bankA, bankB]
        with stD:
            wldD = sbD("wldD", (128, NE * 512))
            wbfD = sbD("wbfD", (128, NE * 512), BF16)
            ynst = [sbD("ynst0", (128, E), BF16), sbD("ynst1", (128, E), BF16)]
            sstg = [sbD("sstg0", (128, 512), BF16), sbD("sstg1", (128, 512), BF16)]
            t_yuse = [None, None]
            t_suse = [None, None]
            t_wD = None
            for hb in range(4):
                sy.pre(waits=barC + [t_wD])
                t_w = None
                for et in range(NE):
                    t_w = sy.inc(sy.e.dma_start(
                        out=wldD[:, et * 512:(et + 1) * 512],
                        in_=woutT[et * 128:(et + 1) * 128, hb * 512:(hb + 1) * 512]))
                se.pre(reads=("wldD",), writes=("wbfD",), waits=[t_w, t_wD])
                t_wbfD = se.inc(se.e.activation(out=wbfD[:], in_=wldD[:], func=AF.Copy))
                for tt in range(NT):
                    sl = tt % 2
                    sy.pre(waits=barC + [t_yuse[sl]])
                    t_y = None
                    for et in range(NE):
                        t_y = sy.inc(sy.e.dma_start(
                            out=ynst[sl][:, et * 128:(et + 1) * 128],
                            in_=ynT_scr[et * 128:(et + 1) * 128, tt * 128:(tt + 1) * 128]))
                    te.pre(waits=[t_wbfD, t_y, t_pmm[sl]])
                    for et in range(NE):
                        mmi = te.e.matmul(out=pdd2D[sl][:, 0:512],
                                          lhsT=ynst[sl][:, et * 128:(et + 1) * 128],
                                          rhs=wbfD[:, et * 512:(et + 1) * 512],
                                          start=(et == 0), stop=(et == NE - 1))
                    tmm = te.inc(mmi)
                    t_yuse[sl] = tmm
                    se.pre(writes=("sstg%d" % sl,), waits=[tmm, t_suse[sl]])
                    tcp = se.inc(se.e.activation(out=sstg[sl][:], in_=pmm[sl][:, 0:512],
                                                 func=AF.Copy))
                    t_pmm[sl] = tcp
                    gp.pre(waits=[tcp])
                    t_suse[sl] = gp.inc(gp.e.dma_start(
                        out=s_scr[tt * 128:(tt + 1) * 128, hb * 512:(hb + 1) * 512],
                        in_=sstg[sl][:]))
                t_wD = t_pmm[1] if NT > 1 else t_pmm[0]
        barD = barrier()

        stE = contextlib.ExitStack()
        sbE = mk(stE)
        with stE:
            qwdH = sbE("qwdH", (128, NI * 1024), BF16)
            dqdB = sbE("dqdB", (128, H))
            qsD = sbE("qsD", (128, 16))
            colE = sbE("colE", (128, 8))
            dqrowE = sbE("dqrowE", (1, 128))
            t_out_final = None
            t_qwd_used = None
            barH = barrier()
            for half in range(2):
                stE1 = contextlib.ExitStack()
                sbE1 = mk(stE1)
                with stE1:
                    nslE = [sbE1("nslE0_%d" % half, (128, IFF)), sbE1("nslE1_%d" % half, (128, IFF))]
                    waE = sbE1("waE_%d" % half, (128, 1408))
                    wbE = sbE1("wbE_%d" % half, (128, 1408))
                    wcE = sbE1("wcE_%d" % half, (128, 1408))
                    wiE = sbE1("wiE_%d" % half, (128, 1408), I32)
                    qnatE = sbE1("qnatE_%d" % half, (128, IFF), BF16)
                    t_nuse = [None, None]
                    for jj in range(8):
                        j = half * 8 + jj
                        sl = jj % 2
                        sy.pre(waits=barH + [t_nuse[sl], t_qwd_used])
                        t_w = sy.inc(sy.e.dma_start(out=nslE[sl][:],
                                                    in_=wdN[j * 128:(j + 1) * 128, :]))
                        kn = "nslE%d" % sl
                        ve.pre(reads=(kn,), writes=("colE",), waits=[t_w])
                        ve.e.reduce_max(out=colE[:, 0:1], in_=nslE[sl][:], axis=AX.X,
                                        apply_absolute_value=True)
                        ve.pre(reads=("colE",), writes=("colE", "qsD"))
                        ve.e.tensor_scalar(out=qsD[:, j:j + 1], in0=colE[:, 0:1],
                                           scalar1=1e-30, scalar2=None, op0=ALU.max)
                        ve.pre(reads=("qsD",), writes=("qsD",))
                        ve.e.reciprocal(out=qsD[:, j:j + 1], in_=qsD[:, j:j + 1])
                        ve.pre(reads=("qsD",), writes=("qsD",))
                        ve.e.tensor_scalar(out=qsD[:, j:j + 1], in0=qsD[:, j:j + 1],
                                           scalar1=448.0, scalar2=None, op0=ALU.mult)
                        ve.pre(reads=("colE",), writes=("colE",))
                        tv_dq = ve.inc(ve.e.tensor_scalar(out=colE[:, 2:3], in0=colE[:, 0:1],
                                                          scalar1=float(1.0 / 448.0),
                                                          scalar2=None, op0=ALU.mult))
                        te.pre(waits=[tv_dq])
                        tm = te.inc(te.e.transpose(out=ptrf[0:1, 0:128], in_=colE[:, 2:3],
                                                   identity=identf[:]))
                        ve.pre(writes=("dqrowE",), waits=[tm])
                        tv = ve.inc(ve.e.tensor_copy(out=dqrowE[:], in_=ptrf[0:1, 0:128]))
                        te.pre(waits=[tv])
                        tm = te.inc(te.e.matmul(out=pbc, lhsT=ones[0:1, :],
                                                rhs=dqrowE[:], start=True, stop=True))
                        ve.pre(writes=("dqdB",), waits=[tm])
                        ve.inc(ve.e.tensor_copy(out=dqdB[:, j * 128:(j + 1) * 128],
                                                in_=pbc))
                        t_q = None
                        for qc in range(4):
                            csl = slice(qc * 1408, (qc + 1) * 1408)
                            ve.pre(reads=(kn, "qsD"), writes=(kn,))
                            tvp = ve.inc(ve.e.tensor_scalar(out=nslE[sl][:, csl],
                                                            in0=nslE[sl][:, csl],
                                                            scalar1=qsD[:, j:j + 1],
                                                            scalar2=None, op0=ALU.mult))
                            t_q = fp8q(nslE[sl][:, csl], qnatE[:, csl], waE[:], wbE[:],
                                       wcE[:], wiE[:], kn, "qnatE", waits=[tvp])
                        t_nuse[sl] = t_q
                        for kt in range(NI):
                            s2 = kt % 2
                            te.pre(waits=[t_q, t_tr[s2], t_qwd_used])
                            tm = te.inc(te.e.transpose(
                                out=ptrb[s2][:],
                                in_=qnatE[:, kt * 128:(kt + 1) * 128],
                                identity=ident[:]))
                            se.pre(writes=("qwdH",), waits=[tm])
                            t_tr[s2] = se.inc(se.e.activation(func=AF.Copy, 
                                out=qwdH[:, kt * 1024 + jj * 128: kt * 1024 + (jj + 1) * 128],
                                in_=ptrb[s2][:]))
                    t_qwd = se.tok()
                barE1 = barrier()
                stE2 = contextlib.ExitStack()
                sbE2 = mk(stE2)
                with stE2:
                    Tld = sbE2("Tld_%d" % half, (128, IFF))
                    waT = sbE2("waT_%d" % half, (128, 1408))
                    wbT = sbE2("wbT_%d" % half, (128, 1408))
                    wcT = sbE2("wcT_%d" % half, (128, 1408))
                    wiT = sbE2("wiT_%d" % half, (128, 1408), I32)
                    qtb = sbE2("qtb_%d" % half, (128, IFF), BF16)
                    qtTb = sbE2("qtTb_%d" % half, (128, IFF), BF16)
                    xld = sbE2("xld_%d" % half, (128, 1024))
                    sld = sbE2("sld_%d" % half, (128, 1024), BF16)
                    gbuf = sbE2("gbuf_%d" % half, (128, 256))
                    sgb = sbE2("sgb_%d" % half, (128, 256), BF16)
                    outb = [sbE2("outb0_%d" % half, (128, 1024)), sbE2("outb1_%d" % half, (128, 1024))]
                    t_Tuse = None
                    t_qt_dma = None
                    t_ouse = [None, None]
                    t_xuseE = None
                    t_qtT = None
                    t_dn = None
                    for tt in range(NT):
                        osl = tt % 2
                        if half == 0:
                            sy.pre(waits=barE1 + [t_Tuse])
                            t_T = sy.inc(sy.e.dma_start(
                                out=Tld[:], in_=T_scr[tt * 128:(tt + 1) * 128, :]))
                            t_q = None
                            for qc in range(4):
                                csl = slice(qc * 1408, (qc + 1) * 1408)
                                ve.pre(reads=("Tld", "r2q"), writes=("Tld",),
                                       waits=[t_T, t_qtT])
                                tvp = ve.inc(ve.e.tensor_scalar(out=Tld[:, csl],
                                                                in0=Tld[:, csl],
                                                                scalar1=r2q[:, tt:tt + 1],
                                                                scalar2=None, op0=ALU.mult))
                                t_q = fp8q(Tld[:, csl], qtb[:, csl], waT[:], wbT[:],
                                           wcT[:], wiT[:], "Tld", "qtb", waits=[tvp])
                            t_Tuse = t_q
                            for kt in range(NI):
                                s2 = kt % 2
                                te.pre(waits=[t_q, t_tr[s2]])
                                tm = te.inc(te.e.transpose(
                                    out=ptrb[s2][:],
                                    in_=qtb[:, kt * 128:(kt + 1) * 128],
                                    identity=ident[:]))
                                se.pre(writes=("qtTb",), waits=[tm, t_qt_dma, t_dn])
                                t_tr[s2] = se.inc(se.e.activation(func=AF.Copy, 
                                    out=qtTb[:, kt * 128:(kt + 1) * 128],
                                    in_=ptrb[s2][:]))
                            t_qtT = t_tr[(NI - 1) % 2]
                            gp.pre(waits=[t_qtT])
                            t_qt_dma = gp.inc(gp.e.dma_start(
                                out=qt_scr[tt * 128:(tt + 1) * 128, :], in_=qtTb[:]))
                        else:
                            sy.pre(waits=barE1 + [t_qtT, t_qt_dma, t_dn])
                            t_qtT = sy.inc(sy.e.dma_start(
                                out=qtTb[:], in_=qt_scr[tt * 128:(tt + 1) * 128, :]))
                        sy.pre(waits=[t_xuseE])
                        sy.inc(sy.e.dma_start(
                            out=xld[:], in_=xin[tt * 128:(tt + 1) * 128,
                                               half * 1024:(half + 1) * 1024]))
                        t_x = sy.inc(sy.e.dma_start(
                            out=sld[:], in_=s_scr[tt * 128:(tt + 1) * 128,
                                                  half * 1024:(half + 1) * 1024]))
                        for hb in range(4):
                            p2 = hb % 2
                            te.pre(waits=[t_qwd, t_qtT if half else t_tr[0],
                                          t_tr[1], t_pmm[p2]])
                            for kt in range(NI):
                                mmi = te.e.matmul(
                                    out=pmm[p2][:, 0:256],
                                    lhsT=qtTb[:, kt * 128:(kt + 1) * 128],
                                    rhs=qwdH[:, kt * 1024 + hb * 256: kt * 1024 + (hb + 1) * 256],
                                    start=(kt == 0), stop=(kt == NI - 1))
                            tmm = te.inc(mmi)
                            ve.pre(reads=("dqdB", "r2sc"), writes=("gbuf",), waits=[tmm])
                            tv = ve.inc(ve.e.scalar_tensor_tensor(
                                out=gbuf[:], in0=pmm[p2][:, 0:256],
                                scalar=r2sc[:, tt:tt + 1],
                                in1=dqdB[:, half * 1024 + hb * 256: half * 1024 + (hb + 1) * 256],
                                op0=ALU.mult, op1=ALU.mult))
                            t_pmm[p2] = tv
                            se.pre(reads=("gbuf",), writes=("gbuf",), waits=[tv])
                            tsl = se.inc(se.e.activation(out=gbuf[:], in_=gbuf[:],
                                                         func=AF.Silu))
                            ve.pre(reads=("gbuf", "sld"), writes=("sgb",),
                                   waits=[tsl, t_x])
                            ve.e.tensor_tensor(out=sgb[:],
                                               in0=gbuf[:],
                                               in1=sld[:, hb * 256:(hb + 1) * 256],
                                               op=ALU.mult)
                            ve.pre(reads=("sgb", "xld"), writes=("outb%d" % osl,),
                                   waits=[t_ouse[osl]])
                            tv = ve.inc(ve.e.tensor_tensor(
                                out=outb[osl][:, hb * 256:(hb + 1) * 256],
                                in0=xld[:, hb * 256:(hb + 1) * 256],
                                in1=sgb[:], op=ALU.add))
                            if hb == 3:
                                t_xuseE = tv
                        t_dn = te.tok()
                        gp.pre(waits=[t_xuseE, t_ouse[osl]])
                        t_ouse[osl] = gp.inc(gp.e.dma_start(
                            out=out[tt * 128:(tt + 1) * 128,
                                    half * 1024:(half + 1) * 1024],
                            in_=outb[osl][:]))
                    t_qwd_used = te.tok()
                    t_out_final = t_ouse[(NT - 1) % 2]
                barH = barrier()
            gp.pre()
            gp.wait((sem_st, gp.n))
    nc.finalize()
    return nc


def kernel(L0_attn_x, L0_attn_in_proj_w, L0_attn_conv_w, L0_attn_conv_b,
           L0_attn_dt_bias, L0_attn_A_log, L0_attn_D, L0_attn_norm_w,
           L0_attn_out_proj_w, L0_ffn_w_gate, L0_ffn_w_up, L0_ffn_w_down,
           L0_ffn_up_reorder_index, L0_ffn_down_reorder_index):
    f32 = np.float32
    x = np.ascontiguousarray(np.asarray(L0_attn_x, f32))
    win = np.asarray(L0_attn_in_proj_w, f32)
    wout = np.asarray(L0_attn_out_proj_w, f32)
    wg = np.asarray(L0_ffn_w_gate, f32)
    wu = np.asarray(L0_ffn_w_up, f32)
    wd = np.asarray(L0_ffn_w_down, f32)
    dn = np.asarray(L0_ffn_down_reorder_index)
    pi = np.take(dn, dn)

    winT = np.ascontiguousarray(win.T)
    woutT = np.ascontiguousarray(wout.T)
    wgN = np.ascontiguousarray(np.take(wg, pi, axis=0))
    wuN = np.ascontiguousarray(np.take(wu, pi, axis=0))
    wdN = np.ascontiguousarray(np.take(wd, dn, axis=1))
    convw = np.ascontiguousarray(np.asarray(L0_attn_conv_w, f32))
    convb = np.ascontiguousarray(np.asarray(L0_attn_conv_b, f32).reshape(CONV, 1))
    dtb = np.asarray(L0_attn_dt_bias, f32)
    alog = np.asarray(L0_attn_A_log, f32)
    dv = np.asarray(L0_attn_D, f32)
    nw = np.asarray(L0_attn_norm_w, f32)

    bf = ml_dtypes.bfloat16
    ident = np.eye(128, dtype=bf)
    identf = np.eye(128, dtype=f32)
    cumlm = np.triu(np.ones((128, 128), f32))
    onesm = np.ones((128, 128), f32)
    ddelta = np.ascontiguousarray(np.repeat(np.eye(NH, dtype=f32), CS, axis=1))

    shared = dict(
        winT=winT, woutT=woutT, wgN=wgN, wuN=wuN, wdN=wdN,
        convw=convw, convb=convb,
        dtb_r=np.ascontiguousarray(dtb.reshape(1, NH)),
        dtb_c=np.ascontiguousarray(dtb.reshape(NH, 1)),
        alog_r=np.ascontiguousarray(alog.reshape(1, NH)),
        alog_c=np.ascontiguousarray(alog.reshape(NH, 1)),
        d_r=np.ascontiguousarray(dv.reshape(1, NH)),
        nw_r=np.ascontiguousarray(nw.reshape(1, E)),
        ident=ident, identf=identf, cuml=cumlm, ones=onesm, ddelta=ddelta,
    )
    nc = build_nc()
    in_maps = [dict(x=np.ascontiguousarray(x[b]), **shared) for b in range(8)]
    res = run_bass_kernel_spmd(nc, in_maps, core_ids=list(range(8)))
    return np.stack([np.asarray(r["out"], f32) for r in res.results], axis=0)
